# version 41
# baseline (speedup 1.0000x reference)
import numpy as np
import ml_dtypes
from contextlib import ExitStack

import concourse.bass as bass
import concourse.mybir as mybir
from concourse.bass_utils import run_bass_kernel_spmd

F32 = mybir.dt.float32
BF16 = mybir.dt.bfloat16
AF = mybir.ActivationFunctionType
ALU = mybir.AluOpType
AX = mybir.AxisListType

B, L, D = 16, 4096, 1024
NCORES = 8
SEQ_PER_CORE = B // NCORES
TOK = SEQ_PER_CORE * L
G = 256
NSUB = G // 128
NCH = D // 128
NG = TOK // G
GSEQ = L // G
W = 260
LN_EPS = 1e-5
ATT_EPS = 1e-6

ENGS = ("sync", "tensor", "vector", "scalar", "gpsimd")


class Emitter:

    def __init__(self, sems):
        self.sems = sems
        self.cnt = {e: 0 for e in ENGS}
        self.plan = {e: [] for e in ENGS}
        self.last_wait = {e: {} for e in ENGS}
        self.bufs = {}

    def _wait(self, eng, tok):
        prod, val = tok
        if val <= 0:
            return
        lw = self.last_wait[eng]
        if lw.get(prod, -1) >= val:
            return
        lw[prod] = val
        sem = self.sems[prod]
        self.plan[eng].append(lambda e, sem=sem, val=val: e.wait_ge(sem, val))

    def group(self, eng, fns, reads=(), writes=(), dma=False, sem_name=None):
        for b in reads:
            st = self.bufs.get(b)
            if st and st["w"]:
                self._wait(eng, st["w"])
        for b in writes:
            st = self.bufs.get(b)
            if st:
                if st["w"]:
                    self._wait(eng, st["w"])
                for r in st["r"]:
                    self._wait(eng, r)
        step = 16 if dma else 1
        for fn in fns[:-1]:
            self.plan[eng].append(lambda e, fn=fn: fn(e))
        key = sem_name or eng
        self.cnt.setdefault(key, 0)
        self.cnt[key] += step
        sem = self.sems[key]
        self.plan[eng].append(
            lambda e, fn=fns[-1], sem=sem, step=step: fn(e).then_inc(sem, step))
        tok = (key, self.cnt[key])
        for b in reads:
            self.bufs.setdefault(b, {"w": None, "r": []})["r"].append(tok)
        for b in writes:
            st = self.bufs.setdefault(b, {"w": None, "r": []})
            st["w"] = tok
            st["r"] = []
        return tok

    def op(self, eng, fn, reads=(), writes=(), dma=False, sem_name=None):
        return self.group(eng, [fn], reads=reads, writes=writes, dma=dma,
                          sem_name=sem_name)


def build_program():
    nc = bass.Bass("TRN2", target_bir_lowering=False)

    x_hbm = nc.dram_tensor("x", [TOK, D], F32, kind="ExternalInput")
    y_hbm = nc.dram_tensor("y", [TOK, D], F32, kind="ExternalOutput")
    WNAMES = ("wq", "wk", "wv", "wc", "wr", "wk2", "wv2")
    w_hbm = {nm: nc.dram_tensor(nm, [D, D], F32, kind="ExternalInput")
             for nm in WNAMES}
    mur_h = nc.dram_tensor("mur", [128, NCH], F32, kind="ExternalInput")
    muk_h = nc.dram_tensor("muk", [128, NCH], F32, kind="ExternalInput")
    ident_h = nc.dram_tensor("ident", [128, 128], BF16, kind="ExternalInput")

    ctx = ExitStack()
    e = ctx.enter_context
    with ctx:
        wsb = {nm: e(nc.sbuf_tensor(f"W_{nm}", [128, NCH * D], BF16)) for nm in WNAMES}
        wstage = e(nc.sbuf_tensor("wstage", [128, 2 * D], F32))
        lnsc = wstage
        x_tm = [e(nc.sbuf_tensor(f"x_tm{i}", [128, NSUB * D], F32)) for i in range(2)]
        xn_tm = e(nc.sbuf_tensor("xn_tm", [128, NSUB * D], BF16))
        xnf = e(nc.sbuf_tensor("xnf", [128, NCH * W], BF16))
        kbuf = e(nc.sbuf_tensor("kbuf", [128, NCH * W], BF16))
        kvb = e(nc.sbuf_tensor("kvb", [128, NCH * W], BF16))
        ckb = e(nc.sbuf_tensor("ckb", [128, NCH * W], BF16))
        ckv = e(nc.sbuf_tensor("ckv", [128, NCH * W], BF16))
        v16 = e(nc.sbuf_tensor("v16", [128, NCH * 256], BF16))
        rk16 = e(nc.sbuf_tensor("rk16", [128, NCH * 256], BF16))
        tq16 = e(nc.sbuf_tensor("tq16", [128, NCH * 256], BF16))
        tr1p = e(nc.sbuf_tensor("tr1p", [128, NCH * 256], BF16))
        o16 = e(nc.sbuf_tensor("o16", [128, NCH * 256], BF16))
        d1b = e(nc.sbuf_tensor("d1b", [128, NCH * 256], BF16))
        xr16 = e(nc.sbuf_tensor("xr16", [128, NCH * 256], BF16))
        xk16 = e(nc.sbuf_tensor("xk16", [128, NCH * 256], BF16))
        ka16 = e(nc.sbuf_tensor("ka16", [128, NCH * 256], BF16))
        dl16 = e(nc.sbuf_tensor("dl16", [128, NCH * 256], BF16))
        sqsc = e(nc.sbuf_tensor("sqsc", [128, D], BF16))
        ones = e(nc.sbuf_tensor("ones", [128, W], BF16))
        stat = e(nc.sbuf_tensor("stat", [128, 12], F32))
        murs = e(nc.sbuf_tensor("murs", [128, NCH], F32))
        muks = e(nc.sbuf_tensor("muks", [128, NCH], F32))
        idents = e(nc.sbuf_tensor("idents", [128, 128], BF16))
        consts = e(nc.sbuf_tensor("consts", [128, 4], F32))

        psh = [e(nc.psum_tensor(f"psh{i}", [128, 1024], F32)) for i in range(3)]
        pstx = e(nc.psum_tensor("pstx", [128, 2048], BF16))
        pstd = pstx

        sems = {eng: e(nc.semaphore(f"sem_{eng}")) for eng in ENGS}
        sems["sx0"] = e(nc.semaphore("sem_sx0"))
        sems["sx1"] = e(nc.semaphore("sem_sx1"))
        for i in range(4):
            sems[f"sw{i}"] = e(nc.semaphore(f"sem_sw{i}"))
        sem_w0 = e(nc.semaphore("sem_w0"))
        sem_w1 = e(nc.semaphore("sem_w1"))
        sem_w2 = e(nc.semaphore("sem_w2"))
        sem_su = e(nc.semaphore("sem_su"))

        with nc.Block() as blk:
            @blk.sync
            def _(sync):
                sync.dma_start(out=murs[:, :], in_=mur_h[:, :]).then_inc(sem_w2, 16)
                sync.dma_start(out=muks[:, :], in_=muk_h[:, :]).then_inc(sem_w2, 16)
                sync.dma_start(out=idents[:, :], in_=ident_h[:, :]).then_inc(sem_w2, 16)

            @blk.vector
            def _(vector):
                vector.memset(consts[:, 0:1], ATT_EPS)
                nc.const_aps.register(F32, ATT_EPS, consts[:, 0:1])
                vector.memset(consts[:, 1:2], LN_EPS)
                nc.const_aps.register(F32, LN_EPS, consts[:, 1:2])
                vector.memset(consts[:, 2:3], 1.0)
                nc.const_aps.register(F32, 1.0, consts[:, 2:3])
                vector.memset(ones[:, :], 1.0)
                for buf in (xnf, kbuf, kvb, ckb, ckv):
                    vector.memset(buf[:, :], 0.0)

        em = Emitter(sems)

        def fm_data(buf):
            return bass.AP(buf, 4, [[NCH * W, 128], [W, NCH], [1, 256]])

        def fm_shift(buf):
            return bass.AP(buf, 3, [[NCH * W, 128], [W, NCH], [1, 256]])

        def fm_carry(buf):
            return bass.AP(buf, 3, [[NCH * W, 128], [W, NCH], [1, 1]])

        def fm_zc(buf):
            return bass.AP(buf, 2, [[NCH * W, 128], [W, NCH], [1, 2]])

        def fm_last(buf):
            return bass.AP(buf, W - 1, [[NCH * W, 128], [W, NCH], [1, 1]])

        def pl3(buf):
            return bass.AP(buf, 0, [[NCH * 256, 128], [256, NCH], [1, 256]])

        def ps3(ps, n=4):
            return bass.AP(ps, 0, [[1024, 128], [256, n], [1, 256]])

        ps_rr = [0]

        def mm_half(wname, rhs_src, rhs_reads, h):
            pi = ps_rr[0] % 3
            ps_rr[0] += 1
            ps = psh[pi]
            pname = f"psh{pi}"
            fns = []
            for ei in range(4):
                ech = h * 4 + ei
                for d in range(NCH):
                    fns.append(lambda e, ps=ps, ei=ei, ech=ech, d=d, wname=wname:
                               e.matmul(
                                   ps[:, ei * 256:(ei + 1) * 256],
                                   wsb[wname][:, d * D + ech * 128:
                                              d * D + ech * 128 + 128],
                                   rhs_src(d),
                                   start=(d == 0), stop=(d == NCH - 1)))
            em.group("tensor", fns, reads=rhs_reads + (f"W{wname}",),
                     writes=(pname,))
            return pname

        def mm_proj(wname, rhs_src, rhs_reads):
            return [mm_half(wname, rhs_src, rhs_reads, h) for h in range(2)]

        def fe_sync(g):
            xb = x_tm[g % 2]
            xbn = f"x{g % 2}"
            wrs = (xbn, "x1a", "x1b") if g == 1 else (xbn,)
            em.op("sync", lambda e, g=g, xb=xb: e.dma_start(
                out=bass.AP(xb, 0, [[NSUB * D, 128], [D, NSUB], [1, D]]),
                in_=bass.AP(x_hbm, g * G * D, [[D, 128], [128 * D, NSUB], [1, D]])),
                reads=(), writes=wrs, dma=True, sem_name=f"sx{g % 2}")

        def fe_scalar_sumsq(g):
            xb = x_tm[g % 2]
            xbn = f"x{g % 2}"
            for su in range(NSUB):
                em.op("scalar", lambda e, xb=xb, su=su: e.activation(
                    sqsc[:, :], xb[:, su * D:(su + 1) * D], AF.Square,
                    accum_out=stat[:, 2 + su:3 + su]),
                    reads=(xbn,), writes=(f"ssq{su}", "sqsc"))

        def fe_vec_ln(g):
            xb = x_tm[g % 2]
            xbn = f"x{g % 2}"
            for su in range(NSUB):
                em.op("vector", lambda e, xb=xb, su=su: e.reduce_sum(
                    stat[:, 0 + su:1 + su], xb[:, su * D:(su + 1) * D], axis=AX.X),
                    reads=(xbn,), writes=(f"sum{su}",))
            em.op("vector", lambda e: e.tensor_scalar_mul(
                stat[:, 4:6], stat[:, 0:2], 1.0 / D),
                reads=("sum0", "sum1"), writes=("mean",))
            em.op("vector", lambda e: e.tensor_mul(
                stat[:, 6:8], stat[:, 4:6], stat[:, 4:6]),
                reads=("mean",), writes=("msq",))
            em.op("vector", lambda e: e.scalar_tensor_tensor(
                stat[:, 8:10], stat[:, 2:4], 1.0 / D, stat[:, 6:8],
                ALU.mult, ALU.subtract),
                reads=("ssq0", "ssq1", "msq"), writes=("var",))

        def fe_scalar_rstd(g):
            em.op("scalar", lambda e: e.activation(
                stat[:, 10:12], stat[:, 8:10], AF.Ln, bias=LN_EPS),
                reads=("var",), writes=("rstd",))
            em.op("scalar", lambda e: e.activation(
                stat[:, 10:12], stat[:, 10:12], AF.Exp, scale=-0.5),
                reads=("rstd",), writes=("rstd",))

        def fe_vec_xn(g):
            xb = x_tm[g % 2]
            xbn = f"x{g % 2}"
            for su in range(NSUB):
                em.op("vector", lambda e, xb=xb, su=su: e.tensor_scalar(
                    xn_tm[:, su * D:(su + 1) * D], xb[:, su * D:(su + 1) * D],
                    stat[:, 4 + su:5 + su], stat[:, 10 + su:11 + su],
                    ALU.subtract, ALU.mult),
                    reads=(xbn, "mean", "rstd"), writes=("xn_tm",))

        def fe_tensor_tpose(g):
            fns = []
            for c in range(NCH):
                for su in range(NSUB):
                    fns.append(lambda e, c=c, su=su: e.transpose(
                        pstx[:, c * 256 + su * 128: c * 256 + su * 128 + 128],
                        xn_tm[:, su * D + c * 128: su * D + c * 128 + 128],
                        idents[:, :]))
            em.group("tensor", fns, reads=("xn_tm",), writes=("pstx",))

        def front_end_start(g):
            fe_sync(g)
            fe_scalar_sumsq(g)
            fe_vec_ln(g)
            fe_scalar_rstd(g)
            fe_vec_xn(g)
            fe_tensor_tpose(g)

        front_end_start(0)

        def hist_evac(g):
            if g % GSEQ == 0:
                em.op("vector", lambda e: e.memset(fm_zc(xnf), 0.0),
                      reads=(), writes=("xnf",))
            else:
                em.op("scalar", lambda e: e.copy(fm_carry(xnf), fm_last(xnf)),
                      reads=(), writes=("xnf",))
            em.op("scalar", lambda e: e.copy(
                fm_data(xnf),
                bass.AP(pstx, 0, [[2048, 128], [256, NCH], [1, 256]])),
                reads=("pstx",), writes=("xnf",))

        WLOAD = ("wv", "wk", "wq", "wr", "wk2", "wv2", "wc")
        wslot_ap = [wstage[:, 0:D], wstage[:, D:2 * D],
                    x_tm[1][:, 0:D], x_tm[1][:, D:2 * D]]
        wslot_nm = ["wsl0", "wsl1", "x1a", "x1b"]
        k = 0
        for nm in WLOAD:
            scalef = None
            for c in range(NCH):
                slot = k % 4
                em.op("sync", lambda e, nm=nm, c=c, slot=slot: e.dma_start(
                    out=wslot_ap[slot],
                    in_=w_hbm[nm][c * 128:(c + 1) * 128, :]),
                    reads=(), writes=(wslot_nm[slot],), dma=True,
                    sem_name=f"sw{slot}")
                wr = (f"W{nm}",) if c == NCH - 1 else (wslot_nm[slot],)
                src = wslot_ap[slot]
                dst = wsb[nm][:, c * D:(c + 1) * D]
                if k % 2 == 0:
                    if scalef is None:
                        em.op("vector", lambda e, src=src, dst=dst: e.tensor_copy(
                            dst, src), reads=(wslot_nm[slot],), writes=wr)
                    else:
                        em.op("vector", lambda e, src=src, dst=dst, scalef=scalef:
                              e.tensor_scalar_mul(dst, src, scalef),
                              reads=(wslot_nm[slot],), writes=wr)
                else:
                    if scalef is None:
                        em.op("scalar", lambda e, src=src, dst=dst: e.copy(
                            dst, src), reads=(wslot_nm[slot],), writes=wr)
                    else:
                        em.op("scalar", lambda e, src=src, dst=dst, scalef=scalef:
                              e.activation(dst, src, AF.Copy, scale=scalef),
                              reads=(wslot_nm[slot],), writes=wr)
                k += 1
        for nm_ in ("wsl0", "wsl1"):
            st_ = em.bufs.get(nm_)
            if st_:
                al = em.bufs.setdefault("lnsc", {"w": None, "r": []})
                if st_["w"]:
                    al["r"].append(st_["w"])
                al["r"].extend(st_["r"])

        hist_evac(0)

        for g in range(NG):
            gi = g % GSEQ
            xb = x_tm[g % 2]
            xbn = f"x{g % 2}"
            if g + 1 < NG:
                fe_sync(g + 1)

            em.op("vector", lambda e: e.tensor_sub(
                pl3(d1b), fm_data(xnf), fm_shift(xnf)),
                reads=("xnf",), writes=("d1",))
            fns = [lambda e, c=c: e.scalar_tensor_tensor(
                xr16[:, c * 256:(c + 1) * 256], d1b[:, c * 256:(c + 1) * 256],
                murs[:, c:c + 1],
                bass.AP(xnf, c * W + 3, [[NCH * W, 128], [1, 256]]),
                ALU.mult, ALU.add) for c in range(NCH)]
            em.group("vector", fns, reads=("d1", "xnf"), writes=("xr",))
            fns = [lambda e, c=c: e.scalar_tensor_tensor(
                xk16[:, c * 256:(c + 1) * 256], d1b[:, c * 256:(c + 1) * 256],
                muks[:, c:c + 1],
                bass.AP(xnf, c * W + 3, [[NCH * W, 128], [1, 256]]),
                ALU.mult, ALU.add) for c in range(NCH)]
            em.group("vector", fns, reads=("d1", "xnf"), writes=("xk",))

            xnf_rhs = lambda d: bass.AP(xnf, d * W + 4, [[NCH * W, 128], [1, 256]])
            pv = mm_proj("wv", xnf_rhs, ("xnf",))
            for h in range(2):
                pn = pv[h]
                em.op("scalar", lambda e, pn=pn, h=h: e.copy(
                    v16[:, h * 1024:(h + 1) * 1024], psh[int(pn[-1])][:, :]),
                    reads=(pn,), writes=("v16",))
            pk = mm_proj("wk", xnf_rhs, ("xnf",))
            if gi == 0:
                em.op("vector", lambda e: e.memset(fm_zc(kbuf), 0.0),
                      reads=(), writes=("kbuf",))
                em.op("vector", lambda e: e.memset(fm_zc(kvb), 0.0),
                      reads=(), writes=("kvb",))
            else:
                em.op("scalar", lambda e: e.copy(fm_carry(kbuf), fm_last(ckb)),
                      reads=("ckb",), writes=("kbuf",))
                em.op("vector", lambda e: e.tensor_copy(fm_carry(kvb), fm_last(ckv)),
                      reads=("ckv",), writes=("kvb",))
            for h in range(2):
                pn = pk[h]
                em.op("scalar", lambda e, pn=pn, h=h: e.activation(
                    bass.AP(kbuf, h * 4 * W + 4, [[NCH * W, 128], [W, 4], [1, 256]]),
                    ps3(psh[int(pn[-1])]), AF.Exp),
                    reads=(pn,), writes=("kbuf",))
            em.op("vector", lambda e: e.tensor_mul(
                fm_data(kvb), fm_data(kbuf), pl3(v16)),
                reads=("kbuf", "v16"), writes=("kvb",))
            for c in range(NCH):
                em.op("vector", lambda e, c=c: e.tensor_tensor_scan(
                    out=ckb[:, c * W + 2:(c + 1) * W],
                    data0=ones[:, 0:W - 2], data1=kbuf[:, c * W + 2:(c + 1) * W],
                    initial=0.0, op0=ALU.mult, op1=ALU.add),
                    reads=("kbuf",) if c == 0 else (), writes=(f"ckb{c}",))
            em.op("scalar", lambda e: e.activation(
                pl3(lnsc), fm_data(ckb), AF.Ln, bias=ATT_EPS),
                reads=("ckb0", "ckb1", "ckb2", "ckb3", "ckb4", "ckb5", "ckb6", "ckb7",), writes=("ckb", "lnsc"))
            em.op("scalar", lambda e: e.activation(
                rk16[:, :], lnsc[:, :], AF.Exp, scale=-1.0),
                reads=("lnsc",), writes=("rk",))
            for c in range(NCH):
                em.op("vector", lambda e, c=c: e.tensor_tensor_scan(
                    out=ckv[:, c * W + 2:(c + 1) * W],
                    data0=ones[:, 0:W - 2], data1=kvb[:, c * W + 2:(c + 1) * W],
                    initial=0.0, op0=ALU.mult, op1=ALU.add),
                    reads=("kvb",) if c == 0 else (), writes=(f"ckv{c}",))

            pq = mm_proj("wq", xnf_rhs, ("xnf",))
            for h in range(2):
                pn = pq[h]
                em.op("scalar", lambda e, pn=pn, h=h: e.activation(
                    tq16[:, h * 1024:(h + 1) * 1024], psh[int(pn[-1])][:, :],
                    AF.Exp, scale=-1.0),
                    reads=(pn,), writes=("tq",))
            em.op("scalar", lambda e: e.activation(
                tq16[:, :], tq16[:, :], AF.Ln, bias=1.0),
                reads=(), writes=("tq",))
            em.op("scalar", lambda e: e.activation(
                tq16[:, :], tq16[:, :], AF.Exp, scale=-1.0),
                reads=(), writes=("tq",))
            pr = mm_proj("wr", lambda d: xr16[:, d * 256:(d + 1) * 256], ("xr",))
            for h in range(2):
                pn = pr[h]
                em.op("scalar", lambda e, pn=pn, h=h: e.activation(
                    tr1p[:, h * 1024:(h + 1) * 1024], psh[int(pn[-1])][:, :],
                    AF.Exp, scale=-1.0),
                    reads=(pn,), writes=("tr1p",))
            em.op("scalar", lambda e: e.activation(
                tr1p[:, :], tr1p[:, :], AF.Ln, bias=1.0),
                reads=(), writes=("tr1p",))
            em.op("scalar", lambda e: e.activation(
                tr1p[:, :], tr1p[:, :], AF.Exp, scale=-1.0),
                reads=(), writes=("tr1p",))
            pk2 = []
            for h in range(2):
                pk2.append(mm_half("wk2", lambda d: xk16[:, d * 256:(d + 1) * 256],
                                   ("xk",), h))
                pn = pk2[h]
                em.op("vector", lambda e, pn=pn, h=h: e.tensor_scalar_max(
                    ka16[:, h * 1024:(h + 1) * 1024], psh[int(pn[-1])][:, :], 0.0),
                    reads=(pn,), writes=("ka",))
                em.op("vector", lambda e, h=h: e.tensor_mul(
                    ka16[:, h * 1024:(h + 1) * 1024],
                    ka16[:, h * 1024:(h + 1) * 1024],
                    ka16[:, h * 1024:(h + 1) * 1024]),
                    reads=(), writes=("ka",))

            if g + 1 < NG:
                fe_scalar_sumsq(g + 1)
                fe_vec_ln(g + 1)

            if g + 1 < NG:
                fe_scalar_rstd(g + 1)

            em.op("vector", lambda e: e.tensor_mul(
                pl3(d1b), fm_data(ckv), pl3(rk16)),
                reads=("ckv0", "ckv1", "ckv2", "ckv3", "ckv4", "ckv5", "ckv6", "ckv7", "rk"), writes=("ckv", "d1"))
            em.op("vector", lambda e: e.tensor_mul(
                pl3(o16), pl3(tq16), pl3(d1b)),
                reads=("tq", "d1"), writes=("o16",))

            pf = []
            for h in range(2):
                pf.append(mm_half("wv2", lambda d: ka16[:, d * 256:(d + 1) * 256],
                                  ("ka",), h))
                fn_ = pf[h]
                em.op("vector", lambda e, fn_=fn_, h=h: e.tensor_mul(
                    tq16[:, h * 1024:(h + 1) * 1024], psh[int(fn_[-1])][:, :],
                    tr1p[:, h * 1024:(h + 1) * 1024]),
                    reads=(fn_, "tr1p"), writes=("tq",))
            if g + 1 < NG:
                fe_vec_xn(g + 1)
                fe_tensor_tpose(g + 1)
                hist_evac(g + 1)
            for h in range(2):
                an_ = mm_half("wc", lambda d: o16[:, d * 256:(d + 1) * 256],
                              ("o16",), h)
                em.op("vector", lambda e, an_=an_, h=h: e.tensor_add(
                    dl16[:, h * 1024:(h + 1) * 1024],
                    tq16[:, h * 1024:(h + 1) * 1024], psh[int(an_[-1])][:, :]),
                    reads=(an_, "tq"), writes=("dl",))

            fns = []
            for c in range(NCH):
                for su in range(NSUB):
                    fns.append(lambda e, c=c, su=su: e.transpose(
                        pstd[:, su * D + c * 128: su * D + c * 128 + 128],
                        dl16[:, c * 256 + su * 128: c * 256 + su * 128 + 128],
                        idents[:, :]))
            em.group("tensor", fns, reads=("dl",), writes=("pstx",))
            em.op("vector", lambda e, xb=xb: e.tensor_add(
                xb[:, :], xb[:, :], pstd[:, :]),
                reads=("pstx",), writes=(xbn,))
            em.op("gpsimd", lambda e, g=g, xb=xb: e.dma_start(
                out=bass.AP(y_hbm, g * G * D, [[D, 128], [128 * D, NSUB], [1, D]]),
                in_=bass.AP(xb, 0, [[NSUB * D, 128], [D, NSUB], [1, D]])),
                reads=(xbn,), writes=(), dma=True)

        total_store = em.cnt["gpsimd"]

        with nc.Block() as blk:
            @blk.sync
            def _(eng):
                for f in em.plan["sync"]:
                    f(eng)

            @blk.tensor
            def _(eng):
                eng.wait_ge(sem_w2, 48)
                for f in em.plan["tensor"]:
                    f(eng)

            @blk.vector
            def _(eng):
                for f in em.plan["vector"]:
                    f(eng)

            @blk.scalar
            def _(eng):
                for f in em.plan["scalar"]:
                    f(eng)

            @blk.gpsimd
            def _(eng):
                for f in em.plan["gpsimd"]:
                    f(eng)
                eng.wait_ge(sems["gpsimd"], total_store)

    return nc


_NC_CACHE = [None]


def kernel(L0_attn_X, L0_attn_Wq, L0_attn_Wk, L0_attn_Wv, L0_attn_Wc,
           L0_ffn_W_r, L0_ffn_W_k, L0_ffn_W_v, L0_ffn_mu_r, L0_ffn_mu_k):
    X = np.ascontiguousarray(np.asarray(L0_attn_X, dtype=np.float32))
    weights = {
        "wq": np.ascontiguousarray(np.asarray(L0_attn_Wq, np.float32)),
        "wk": np.ascontiguousarray(np.asarray(L0_attn_Wk, np.float32)),
        "wv": np.ascontiguousarray(np.asarray(L0_attn_Wv, np.float32)),
        "wc": np.ascontiguousarray(np.asarray(L0_attn_Wc, np.float32)),
        "wr": np.ascontiguousarray(np.asarray(L0_ffn_W_r, np.float32).T),
        "wk2": np.ascontiguousarray(np.asarray(L0_ffn_W_k, np.float32).T),
        "wv2": np.ascontiguousarray(np.asarray(L0_ffn_W_v, np.float32).T),
    }
    mur = np.ascontiguousarray(np.asarray(L0_ffn_mu_r, np.float32).reshape(NCH, 128).T)
    muk = np.ascontiguousarray(np.asarray(L0_ffn_mu_k, np.float32).reshape(NCH, 128).T)
    ident = np.eye(128, dtype=ml_dtypes.bfloat16)

    if _NC_CACHE[0] is None:
        _NC_CACHE[0] = build_program()
    nc = _NC_CACHE[0]

    in_maps = []
    for i in range(NCORES):
        xi = np.ascontiguousarray(
            X[SEQ_PER_CORE * i: SEQ_PER_CORE * (i + 1)].reshape(TOK, D))
        m = {"x": xi, "mur": mur, "muk": muk, "ident": ident}
        m.update(weights)
        in_maps.append(m)

    res = run_bass_kernel_spmd(nc, in_maps, core_ids=list(range(NCORES)))
    out = np.empty((B, L, D), dtype=np.float32)
    for i in range(NCORES):
        out[SEQ_PER_CORE * i: SEQ_PER_CORE * (i + 1)] = np.asarray(
            res.results[i]["y"]).reshape(SEQ_PER_CORE, L, D)
    return out


# version 44
# speedup vs baseline: 1.0128x; 1.0128x over previous
import numpy as np
import ml_dtypes
from contextlib import ExitStack

import concourse.bass as bass
import concourse.mybir as mybir
from concourse.bass_utils import run_bass_kernel_spmd

F32 = mybir.dt.float32
BF16 = mybir.dt.bfloat16
AF = mybir.ActivationFunctionType
ALU = mybir.AluOpType
AX = mybir.AxisListType

B, L, D = 16, 4096, 1024
NCORES = 8
SEQ_PER_CORE = B // NCORES
TOK = SEQ_PER_CORE * L
G = 256
NSUB = G // 128
NCH = D // 128
NG = TOK // G
GSEQ = L // G
W = 260
LN_EPS = 1e-5
ATT_EPS = 1e-6

ENGS = ("sync", "tensor", "vector", "scalar", "gpsimd")


class Emitter:

    def __init__(self, sems):
        self.sems = sems
        self.cnt = {e: 0 for e in ENGS}
        self.plan = {e: [] for e in ENGS}
        self.last_wait = {e: {} for e in ENGS}
        self.bufs = {}

    def _wait(self, eng, tok):
        prod, val = tok
        if val <= 0:
            return
        lw = self.last_wait[eng]
        if lw.get(prod, -1) >= val:
            return
        lw[prod] = val
        sem = self.sems[prod]
        self.plan[eng].append(lambda e, sem=sem, val=val: e.wait_ge(sem, val))

    def group(self, eng, fns, reads=(), writes=(), dma=False, sem_name=None):
        for b in reads:
            st = self.bufs.get(b)
            if st and st["w"]:
                self._wait(eng, st["w"])
        for b in writes:
            st = self.bufs.get(b)
            if st:
                if st["w"]:
                    self._wait(eng, st["w"])
                for r in st["r"]:
                    self._wait(eng, r)
        step = 16 if dma else 1
        for fn in fns[:-1]:
            self.plan[eng].append(lambda e, fn=fn: fn(e))
        key = sem_name or eng
        self.cnt.setdefault(key, 0)
        self.cnt[key] += step
        sem = self.sems[key]
        self.plan[eng].append(
            lambda e, fn=fns[-1], sem=sem, step=step: fn(e).then_inc(sem, step))
        tok = (key, self.cnt[key])
        for b in reads:
            self.bufs.setdefault(b, {"w": None, "r": []})["r"].append(tok)
        for b in writes:
            st = self.bufs.setdefault(b, {"w": None, "r": []})
            st["w"] = tok
            st["r"] = []
        return tok

    def op(self, eng, fn, reads=(), writes=(), dma=False, sem_name=None):
        return self.group(eng, [fn], reads=reads, writes=writes, dma=dma,
                          sem_name=sem_name)


def build_program():
    nc = bass.Bass("TRN2", target_bir_lowering=False)

    x_hbm = nc.dram_tensor("x", [TOK, D], F32, kind="ExternalInput")
    y_hbm = nc.dram_tensor("y", [TOK, D], F32, kind="ExternalOutput")
    WNAMES = ("wq", "wk", "wv", "wc", "wr", "wk2", "wv2")
    w_hbm = {nm: nc.dram_tensor(nm, [D, D], F32, kind="ExternalInput")
             for nm in WNAMES}
    mur_h = nc.dram_tensor("mur", [128, NCH], F32, kind="ExternalInput")
    muk_h = nc.dram_tensor("muk", [128, NCH], F32, kind="ExternalInput")
    ident_h = nc.dram_tensor("ident", [128, 128], BF16, kind="ExternalInput")

    ctx = ExitStack()
    e = ctx.enter_context
    with ctx:
        wsb = {nm: e(nc.sbuf_tensor(f"W_{nm}", [128, NCH * D], BF16)) for nm in WNAMES}
        wstage = e(nc.sbuf_tensor("wstage", [128, 2 * D], F32))
        lnsc = wstage
        x_tm = [e(nc.sbuf_tensor(f"x_tm{i}", [128, NSUB * D], F32)) for i in range(2)]
        xn_tm = e(nc.sbuf_tensor("xn_tm", [128, NSUB * D], BF16))
        xnf = e(nc.sbuf_tensor("xnf", [128, NCH * W], BF16))
        kbuf = e(nc.sbuf_tensor("kbuf", [128, NCH * W], BF16))
        kvb = e(nc.sbuf_tensor("kvb", [128, NCH * W], BF16))
        ckb = e(nc.sbuf_tensor("ckb", [128, NCH * W], BF16))
        ckv = e(nc.sbuf_tensor("ckv", [128, NCH * W], BF16))
        v16 = e(nc.sbuf_tensor("v16", [128, NCH * 256], BF16))
        rk16 = e(nc.sbuf_tensor("rk16", [128, NCH * 256], BF16))
        tq16 = e(nc.sbuf_tensor("tq16", [128, NCH * 256], BF16))
        tr1p = e(nc.sbuf_tensor("tr1p", [128, NCH * 256], BF16))
        o16 = e(nc.sbuf_tensor("o16", [128, NCH * 256], BF16))
        d1b = e(nc.sbuf_tensor("d1b", [128, NCH * 256], BF16))
        xr16 = e(nc.sbuf_tensor("xr16", [128, NCH * 256], BF16))
        xk16 = e(nc.sbuf_tensor("xk16", [128, NCH * 256], BF16))
        ka16 = e(nc.sbuf_tensor("ka16", [128, NCH * 256], BF16))
        dl16 = e(nc.sbuf_tensor("dl16", [128, NCH * 256], BF16))
        sqsc = e(nc.sbuf_tensor("sqsc", [128, D], BF16))
        ones = e(nc.sbuf_tensor("ones", [128, W], BF16))
        stat = e(nc.sbuf_tensor("stat", [128, 12], F32))
        murs = e(nc.sbuf_tensor("murs", [128, NCH], F32))
        muks = e(nc.sbuf_tensor("muks", [128, NCH], F32))
        idents = e(nc.sbuf_tensor("idents", [128, 128], BF16))
        consts = e(nc.sbuf_tensor("consts", [128, 4], F32))

        psh = [e(nc.psum_tensor(f"psh{i}", [128, 1024], F32)) for i in range(3)]
        pstx = e(nc.psum_tensor("pstx", [128, 2048], BF16))
        pstd = pstx

        sems = {eng: e(nc.semaphore(f"sem_{eng}")) for eng in ENGS}
        sems["sx0"] = e(nc.semaphore("sem_sx0"))
        sems["sx1"] = e(nc.semaphore("sem_sx1"))
        for i in range(4):
            sems[f"sw{i}"] = e(nc.semaphore(f"sem_sw{i}"))
        sem_w0 = e(nc.semaphore("sem_w0"))
        sem_w1 = e(nc.semaphore("sem_w1"))
        sem_w2 = e(nc.semaphore("sem_w2"))
        sem_su = e(nc.semaphore("sem_su"))

        with nc.Block() as blk:
            @blk.sync
            def _(sync):
                sync.dma_start(out=murs[:, :], in_=mur_h[:, :]).then_inc(sem_w2, 16)
                sync.dma_start(out=muks[:, :], in_=muk_h[:, :]).then_inc(sem_w2, 16)
                sync.dma_start(out=idents[:, :], in_=ident_h[:, :]).then_inc(sem_w2, 16)

            @blk.vector
            def _(vector):
                vector.memset(consts[:, 0:1], ATT_EPS)
                nc.const_aps.register(F32, ATT_EPS, consts[:, 0:1])
                vector.memset(consts[:, 1:2], LN_EPS)
                nc.const_aps.register(F32, LN_EPS, consts[:, 1:2])
                vector.memset(consts[:, 2:3], 1.0)
                nc.const_aps.register(F32, 1.0, consts[:, 2:3])
                vector.memset(ones[:, :], 1.0)
                for buf in (xnf, kbuf, kvb, ckb, ckv):
                    vector.memset(buf[:, :], 0.0)

        em = Emitter(sems)

        def fm_data(buf):
            return bass.AP(buf, 4, [[NCH * W, 128], [W, NCH], [1, 256]])

        def fm_shift(buf):
            return bass.AP(buf, 3, [[NCH * W, 128], [W, NCH], [1, 256]])

        def fm_carry(buf):
            return bass.AP(buf, 3, [[NCH * W, 128], [W, NCH], [1, 1]])

        def fm_zc(buf):
            return bass.AP(buf, 2, [[NCH * W, 128], [W, NCH], [1, 2]])

        def fm_last(buf):
            return bass.AP(buf, W - 1, [[NCH * W, 128], [W, NCH], [1, 1]])

        def pl3(buf):
            return bass.AP(buf, 0, [[NCH * 256, 128], [256, NCH], [1, 256]])

        def ps3(ps, n=4):
            return bass.AP(ps, 0, [[1024, 128], [256, n], [1, 256]])

        ps_rr = [0]

        def mm_half(wname, rhs_src, rhs_reads, h):
            pi = ps_rr[0] % 3
            ps_rr[0] += 1
            ps = psh[pi]
            pname = f"psh{pi}"
            fns = []
            for ei in range(4):
                ech = h * 4 + ei
                for d in range(NCH):
                    fns.append(lambda e, ps=ps, ei=ei, ech=ech, d=d, wname=wname:
                               e.matmul(
                                   ps[:, ei * 256:(ei + 1) * 256],
                                   wsb[wname][:, d * D + ech * 128:
                                              d * D + ech * 128 + 128],
                                   rhs_src(d),
                                   start=(d == 0), stop=(d == NCH - 1)))
            em.group("tensor", fns, reads=rhs_reads + (f"W{wname}",),
                     writes=(pname,))
            return pname

        def mm_proj(wname, rhs_src, rhs_reads):
            return [mm_half(wname, rhs_src, rhs_reads, h) for h in range(2)]

        def fe_sync(g):
            xb = x_tm[g % 2]
            xbn = f"x{g % 2}"
            wrs = (xbn, "x1a", "x1b") if g == 1 else (xbn,)
            em.op("sync", lambda e, g=g, xb=xb: e.dma_start(
                out=bass.AP(xb, 0, [[NSUB * D, 128], [D, NSUB], [1, D]]),
                in_=bass.AP(x_hbm, g * G * D, [[D, 128], [128 * D, NSUB], [1, D]])),
                reads=(), writes=wrs, dma=True, sem_name=f"sx{g % 2}")

        def fe_scalar_sumsq(g):
            xb = x_tm[g % 2]
            xbn = f"x{g % 2}"
            for su in range(NSUB):
                em.op("scalar", lambda e, xb=xb, su=su: e.activation(
                    sqsc[:, :], xb[:, su * D:(su + 1) * D], AF.Square,
                    accum_out=stat[:, 2 + su:3 + su]),
                    reads=(xbn,), writes=(f"ssq{su}", "sqsc"))

        def fe_vec_ln(g):
            xb = x_tm[g % 2]
            xbn = f"x{g % 2}"
            for su in range(NSUB):
                em.op("vector", lambda e, xb=xb, su=su: e.reduce_sum(
                    stat[:, 0 + su:1 + su], xb[:, su * D:(su + 1) * D], axis=AX.X),
                    reads=(xbn,), writes=(f"sum{su}",))
            em.op("vector", lambda e: e.tensor_scalar_mul(
                stat[:, 4:6], stat[:, 0:2], 1.0 / D),
                reads=("sum0", "sum1"), writes=("mean",))
            em.op("vector", lambda e: e.tensor_mul(
                stat[:, 6:8], stat[:, 4:6], stat[:, 4:6]),
                reads=("mean",), writes=("msq",))
            em.op("vector", lambda e: e.scalar_tensor_tensor(
                stat[:, 8:10], stat[:, 2:4], 1.0 / D, stat[:, 6:8],
                ALU.mult, ALU.subtract),
                reads=("ssq0", "ssq1", "msq"), writes=("var",))

        def fe_scalar_rstd(g):
            em.op("scalar", lambda e: e.activation(
                stat[:, 10:12], stat[:, 8:10], AF.Ln, bias=LN_EPS),
                reads=("var",), writes=("rstd",))
            em.op("scalar", lambda e: e.activation(
                stat[:, 10:12], stat[:, 10:12], AF.Exp, scale=-0.5),
                reads=("rstd",), writes=("rstd",))

        def fe_vec_xn(g):
            xb = x_tm[g % 2]
            xbn = f"x{g % 2}"
            for su in range(NSUB):
                em.op("vector", lambda e, xb=xb, su=su: e.tensor_scalar(
                    xn_tm[:, su * D:(su + 1) * D], xb[:, su * D:(su + 1) * D],
                    stat[:, 4 + su:5 + su], stat[:, 10 + su:11 + su],
                    ALU.subtract, ALU.mult),
                    reads=(xbn, "mean", "rstd"), writes=("xn_tm",))

        def fe_tensor_tpose(g):
            fns = []
            for c in range(NCH):
                for su in range(NSUB):
                    fns.append(lambda e, c=c, su=su: e.transpose(
                        pstx[:, c * 256 + su * 128: c * 256 + su * 128 + 128],
                        xn_tm[:, su * D + c * 128: su * D + c * 128 + 128],
                        idents[:, :]))
            em.group("tensor", fns, reads=("xn_tm",), writes=("pstx",))

        def front_end_start(g):
            fe_sync(g)
            fe_scalar_sumsq(g)
            fe_vec_ln(g)
            fe_scalar_rstd(g)
            fe_vec_xn(g)
            fe_tensor_tpose(g)

        front_end_start(0)

        def hist_evac(g):
            if g % GSEQ == 0:
                em.op("vector", lambda e: e.memset(fm_zc(xnf), 0.0),
                      reads=(), writes=("xnf",))
            else:
                em.op("scalar", lambda e: e.copy(fm_carry(xnf), fm_last(xnf)),
                      reads=(), writes=("xnf",))
            em.op("scalar", lambda e: e.copy(
                fm_data(xnf),
                bass.AP(pstx, 0, [[2048, 128], [256, NCH], [1, 256]])),
                reads=("pstx",), writes=("xnf",))

        WLOAD = ("wv", "wk", "wq", "wr", "wk2", "wv2", "wc")
        wslot_ap = [wstage[:, 0:D], wstage[:, D:2 * D],
                    x_tm[1][:, 0:D], x_tm[1][:, D:2 * D]]
        wslot_nm = ["wsl0", "wsl1", "x1a", "x1b"]
        k = 0
        for nm in WLOAD:
            scalef = None
            for c in range(NCH):
                slot = k % 4
                em.op("sync", lambda e, nm=nm, c=c, slot=slot: e.dma_start(
                    out=wslot_ap[slot],
                    in_=w_hbm[nm][c * 128:(c + 1) * 128, :]),
                    reads=(), writes=(wslot_nm[slot],), dma=True,
                    sem_name=f"sw{slot}")
                wr = (f"W{nm}",) if c == NCH - 1 else (wslot_nm[slot],)
                src = wslot_ap[slot]
                dst = wsb[nm][:, c * D:(c + 1) * D]
                if k % 2 == 0:
                    if scalef is None:
                        em.op("vector", lambda e, src=src, dst=dst: e.tensor_copy(
                            dst, src), reads=(wslot_nm[slot],), writes=wr)
                    else:
                        em.op("vector", lambda e, src=src, dst=dst, scalef=scalef:
                              e.tensor_scalar_mul(dst, src, scalef),
                              reads=(wslot_nm[slot],), writes=wr)
                else:
                    if scalef is None:
                        em.op("scalar", lambda e, src=src, dst=dst: e.copy(
                            dst, src), reads=(wslot_nm[slot],), writes=wr)
                    else:
                        em.op("scalar", lambda e, src=src, dst=dst, scalef=scalef:
                              e.activation(dst, src, AF.Copy, scale=scalef),
                              reads=(wslot_nm[slot],), writes=wr)
                k += 1
        for nm_ in ("wsl0", "wsl1"):
            st_ = em.bufs.get(nm_)
            if st_:
                al = em.bufs.setdefault("lnsc", {"w": None, "r": []})
                if st_["w"]:
                    al["r"].append(st_["w"])
                al["r"].extend(st_["r"])

        def v_proj(g):
            rhs_ = lambda d: bass.AP(xnf, d * W + 4, [[NCH * W, 128], [1, 256]])
            pvv = mm_proj("wv", rhs_, ("xnf",))
            for h in range(2):
                pn = pvv[h]
                em.op("scalar", lambda e, pn=pn, h=h: e.copy(
                    v16[:, h * 1024:(h + 1) * 1024], psh[int(pn[-1])][:, :]),
                    reads=(pn,), writes=("v16",))

        hist_evac(0)
        v_proj(0)

        for g in range(NG):
            gi = g % GSEQ
            xb = x_tm[g % 2]
            xbn = f"x{g % 2}"
            if g + 1 < NG:
                fe_sync(g + 1)

            em.op("vector", lambda e: e.tensor_sub(
                pl3(d1b), fm_data(xnf), fm_shift(xnf)),
                reads=("xnf",), writes=("d1",))
            fns = [lambda e, c=c: e.scalar_tensor_tensor(
                xr16[:, c * 256:(c + 1) * 256], d1b[:, c * 256:(c + 1) * 256],
                murs[:, c:c + 1],
                bass.AP(xnf, c * W + 3, [[NCH * W, 128], [1, 256]]),
                ALU.mult, ALU.add) for c in range(NCH)]
            em.group("vector", fns, reads=("d1", "xnf"), writes=("xr",))
            fns = [lambda e, c=c: e.scalar_tensor_tensor(
                xk16[:, c * 256:(c + 1) * 256], d1b[:, c * 256:(c + 1) * 256],
                muks[:, c:c + 1],
                bass.AP(xnf, c * W + 3, [[NCH * W, 128], [1, 256]]),
                ALU.mult, ALU.add) for c in range(NCH)]
            em.group("vector", fns, reads=("d1", "xnf"), writes=("xk",))

            xnf_rhs = lambda d: bass.AP(xnf, d * W + 4, [[NCH * W, 128], [1, 256]])
            pk = mm_proj("wk", xnf_rhs, ("xnf",))
            if gi == 0:
                em.op("vector", lambda e: e.memset(fm_zc(kbuf), 0.0),
                      reads=(), writes=("kbuf",))
                em.op("vector", lambda e: e.memset(fm_zc(kvb), 0.0),
                      reads=(), writes=("kvb",))
            else:
                em.op("scalar", lambda e: e.copy(fm_carry(kbuf), fm_last(ckb)),
                      reads=("ckb",), writes=("kbuf",))
                em.op("vector", lambda e: e.tensor_copy(fm_carry(kvb), fm_last(ckv)),
                      reads=("ckv",), writes=("kvb",))
            for h in range(2):
                pn = pk[h]
                em.op("scalar", lambda e, pn=pn, h=h: e.activation(
                    bass.AP(kbuf, h * 4 * W + 4, [[NCH * W, 128], [W, 4], [1, 256]]),
                    ps3(psh[int(pn[-1])]), AF.Exp),
                    reads=(pn,), writes=("kbuf",))
            em.op("vector", lambda e: e.tensor_mul(
                fm_data(kvb), fm_data(kbuf), pl3(v16)),
                reads=("kbuf", "v16"), writes=("kvb",))
            for c in range(NCH):
                em.op("vector", lambda e, c=c: e.tensor_tensor_scan(
                    out=ckb[:, c * W + 2:(c + 1) * W],
                    data0=ones[:, 0:W - 2], data1=kbuf[:, c * W + 2:(c + 1) * W],
                    initial=0.0, op0=ALU.mult, op1=ALU.add),
                    reads=("kbuf",) if c == 0 else (), writes=(f"ckb{c}",))
            for c in range(NCH):
                em.op("vector", lambda e, c=c: e.tensor_tensor_scan(
                    out=ckv[:, c * W + 2:(c + 1) * W],
                    data0=ones[:, 0:W - 2], data1=kvb[:, c * W + 2:(c + 1) * W],
                    initial=0.0, op0=ALU.mult, op1=ALU.add),
                    reads=("kvb",) if c == 0 else (), writes=(f"ckv{c}",))

            pq = mm_proj("wq", xnf_rhs, ("xnf",))
            for h in range(2):
                pn = pq[h]
                em.op("scalar", lambda e, pn=pn, h=h: e.activation(
                    tq16[:, h * 1024:(h + 1) * 1024], psh[int(pn[-1])][:, :],
                    AF.Exp, scale=-1.0),
                    reads=(pn,), writes=("tq",))
            em.op("scalar", lambda e: e.activation(
                tq16[:, :], tq16[:, :], AF.Ln, bias=1.0),
                reads=(), writes=("tq",))
            em.op("scalar", lambda e: e.activation(
                tq16[:, :], tq16[:, :], AF.Exp, scale=-1.0),
                reads=(), writes=("tq",))
            pr = mm_proj("wr", lambda d: xr16[:, d * 256:(d + 1) * 256], ("xr",))
            for h in range(2):
                pn = pr[h]
                em.op("scalar", lambda e, pn=pn, h=h: e.activation(
                    tr1p[:, h * 1024:(h + 1) * 1024], psh[int(pn[-1])][:, :],
                    AF.Exp, scale=-1.0),
                    reads=(pn,), writes=("tr1p",))
            em.op("scalar", lambda e: e.activation(
                tr1p[:, :], tr1p[:, :], AF.Ln, bias=1.0),
                reads=(), writes=("tr1p",))
            em.op("scalar", lambda e: e.activation(
                tr1p[:, :], tr1p[:, :], AF.Exp, scale=-1.0),
                reads=(), writes=("tr1p",))
            pk2 = []
            for h in range(2):
                pk2.append(mm_half("wk2", lambda d: xk16[:, d * 256:(d + 1) * 256],
                                   ("xk",), h))
                pn = pk2[h]
                em.op("vector", lambda e, pn=pn, h=h: e.tensor_scalar_max(
                    ka16[:, h * 1024:(h + 1) * 1024], psh[int(pn[-1])][:, :], 0.0),
                    reads=(pn,), writes=("ka",))
                em.op("vector", lambda e, h=h: e.tensor_mul(
                    ka16[:, h * 1024:(h + 1) * 1024],
                    ka16[:, h * 1024:(h + 1) * 1024],
                    ka16[:, h * 1024:(h + 1) * 1024]),
                    reads=(), writes=("ka",))

            if g + 1 < NG:
                fe_scalar_sumsq(g + 1)
                fe_vec_ln(g + 1)

            em.op("scalar", lambda e: e.activation(
                pl3(lnsc), fm_data(ckb), AF.Ln, bias=ATT_EPS),
                reads=("ckb0", "ckb1", "ckb2", "ckb3", "ckb4", "ckb5", "ckb6", "ckb7",), writes=("ckb", "lnsc"))
            em.op("scalar", lambda e: e.activation(
                rk16[:, :], lnsc[:, :], AF.Exp, scale=-1.0),
                reads=("lnsc",), writes=("rk",))
            if g + 1 < NG:
                fe_scalar_rstd(g + 1)

            em.op("vector", lambda e: e.tensor_mul(
                pl3(d1b), fm_data(ckv), pl3(rk16)),
                reads=("ckv0", "ckv1", "ckv2", "ckv3", "ckv4", "ckv5", "ckv6", "ckv7", "rk"), writes=("ckv", "d1"))
            em.op("vector", lambda e: e.tensor_mul(
                pl3(o16), pl3(tq16), pl3(d1b)),
                reads=("tq", "d1"), writes=("o16",))

            pf = []
            for h in range(2):
                pf.append(mm_half("wv2", lambda d: ka16[:, d * 256:(d + 1) * 256],
                                  ("ka",), h))
                fn_ = pf[h]
                em.op("vector", lambda e, fn_=fn_, h=h: e.tensor_mul(
                    tq16[:, h * 1024:(h + 1) * 1024], psh[int(fn_[-1])][:, :],
                    tr1p[:, h * 1024:(h + 1) * 1024]),
                    reads=(fn_, "tr1p"), writes=("tq",))
            if g + 1 < NG:
                fe_vec_xn(g + 1)
                fe_tensor_tpose(g + 1)
                hist_evac(g + 1)
            for h in range(2):
                an_ = mm_half("wc", lambda d: o16[:, d * 256:(d + 1) * 256],
                              ("o16",), h)
                em.op("vector", lambda e, an_=an_, h=h: e.tensor_add(
                    dl16[:, h * 1024:(h + 1) * 1024],
                    tq16[:, h * 1024:(h + 1) * 1024], psh[int(an_[-1])][:, :]),
                    reads=(an_, "tq"), writes=("dl",))

            if g + 1 < NG:
                v_proj(g + 1)
            fns = []
            for c in range(NCH):
                for su in range(NSUB):
                    fns.append(lambda e, c=c, su=su: e.transpose(
                        pstd[:, su * D + c * 128: su * D + c * 128 + 128],
                        dl16[:, c * 256 + su * 128: c * 256 + su * 128 + 128],
                        idents[:, :]))
            em.group("tensor", fns, reads=("dl",), writes=("pstx",))
            em.op("vector", lambda e, xb=xb: e.tensor_add(
                xb[:, :], xb[:, :], pstd[:, :]),
                reads=("pstx",), writes=(xbn,))
            em.op("gpsimd", lambda e, g=g, xb=xb: e.dma_start(
                out=bass.AP(y_hbm, g * G * D, [[D, 128], [128 * D, NSUB], [1, D]]),
                in_=bass.AP(xb, 0, [[NSUB * D, 128], [D, NSUB], [1, D]])),
                reads=(xbn,), writes=(), dma=True)

        total_store = em.cnt["gpsimd"]

        with nc.Block() as blk:
            @blk.sync
            def _(eng):
                for f in em.plan["sync"]:
                    f(eng)

            @blk.tensor
            def _(eng):
                eng.wait_ge(sem_w2, 48)
                for f in em.plan["tensor"]:
                    f(eng)

            @blk.vector
            def _(eng):
                for f in em.plan["vector"]:
                    f(eng)

            @blk.scalar
            def _(eng):
                for f in em.plan["scalar"]:
                    f(eng)

            @blk.gpsimd
            def _(eng):
                for f in em.plan["gpsimd"]:
                    f(eng)
                eng.wait_ge(sems["gpsimd"], total_store)

    return nc


_NC_CACHE = [None]


def kernel(L0_attn_X, L0_attn_Wq, L0_attn_Wk, L0_attn_Wv, L0_attn_Wc,
           L0_ffn_W_r, L0_ffn_W_k, L0_ffn_W_v, L0_ffn_mu_r, L0_ffn_mu_k):
    X = np.ascontiguousarray(np.asarray(L0_attn_X, dtype=np.float32))
    weights = {
        "wq": np.ascontiguousarray(np.asarray(L0_attn_Wq, np.float32)),
        "wk": np.ascontiguousarray(np.asarray(L0_attn_Wk, np.float32)),
        "wv": np.ascontiguousarray(np.asarray(L0_attn_Wv, np.float32)),
        "wc": np.ascontiguousarray(np.asarray(L0_attn_Wc, np.float32)),
        "wr": np.ascontiguousarray(np.asarray(L0_ffn_W_r, np.float32).T),
        "wk2": np.ascontiguousarray(np.asarray(L0_ffn_W_k, np.float32).T),
        "wv2": np.ascontiguousarray(np.asarray(L0_ffn_W_v, np.float32).T),
    }
    mur = np.ascontiguousarray(np.asarray(L0_ffn_mu_r, np.float32).reshape(NCH, 128).T)
    muk = np.ascontiguousarray(np.asarray(L0_ffn_mu_k, np.float32).reshape(NCH, 128).T)
    ident = np.eye(128, dtype=ml_dtypes.bfloat16)

    if _NC_CACHE[0] is None:
        _NC_CACHE[0] = build_program()
    nc = _NC_CACHE[0]

    in_maps = []
    for i in range(NCORES):
        xi = np.ascontiguousarray(
            X[SEQ_PER_CORE * i: SEQ_PER_CORE * (i + 1)].reshape(TOK, D))
        m = {"x": xi, "mur": mur, "muk": muk, "ident": ident}
        m.update(weights)
        in_maps.append(m)

    res = run_bass_kernel_spmd(nc, in_maps, core_ids=list(range(NCORES)))
    out = np.empty((B, L, D), dtype=np.float32)
    for i in range(NCORES):
        out[SEQ_PER_CORE * i: SEQ_PER_CORE * (i + 1)] = np.asarray(
            res.results[i]["y"]).reshape(SEQ_PER_CORE, L, D)
    return out


# version 45 (final)
# speedup vs baseline: 1.0879x; 1.0742x over previous
import numpy as np
import ml_dtypes
from contextlib import ExitStack

import concourse.bass as bass
import concourse.mybir as mybir
from concourse.bass_utils import run_bass_kernel_spmd

F32 = mybir.dt.float32
BF16 = mybir.dt.bfloat16
AF = mybir.ActivationFunctionType
ALU = mybir.AluOpType
AX = mybir.AxisListType

B, L, D = 16, 4096, 1024
NCORES = 8
SEQ_PER_CORE = B // NCORES
TOK = SEQ_PER_CORE * L
G = 256
NSUB = G // 128
NCH = D // 128
NG = TOK // G
GSEQ = L // G
W = 260
LN_EPS = 1e-5
ATT_EPS = 1e-6

ENGS = ("sync", "tensor", "vector", "scalar", "gpsimd")


class Emitter:

    def __init__(self, sems):
        self.sems = sems
        self.cnt = {e: 0 for e in ENGS}
        self.plan = {e: [] for e in ENGS}
        self.last_wait = {e: {} for e in ENGS}
        self.bufs = {}

    def _wait(self, eng, tok):
        prod, val = tok
        if val <= 0:
            return
        lw = self.last_wait[eng]
        if lw.get(prod, -1) >= val:
            return
        lw[prod] = val
        sem = self.sems[prod]
        self.plan[eng].append(lambda e, sem=sem, val=val: e.wait_ge(sem, val))

    def group(self, eng, fns, reads=(), writes=(), dma=False, sem_name=None):
        for b in reads:
            st = self.bufs.get(b)
            if st and st["w"]:
                self._wait(eng, st["w"])
        for b in writes:
            st = self.bufs.get(b)
            if st:
                if st["w"]:
                    self._wait(eng, st["w"])
                for r in st["r"]:
                    self._wait(eng, r)
        step = 16 if dma else 1
        for fn in fns[:-1]:
            self.plan[eng].append(lambda e, fn=fn: fn(e))
        key = sem_name or eng
        self.cnt.setdefault(key, 0)
        self.cnt[key] += step
        sem = self.sems[key]
        self.plan[eng].append(
            lambda e, fn=fns[-1], sem=sem, step=step: fn(e).then_inc(sem, step))
        tok = (key, self.cnt[key])
        for b in reads:
            self.bufs.setdefault(b, {"w": None, "r": []})["r"].append(tok)
        for b in writes:
            st = self.bufs.setdefault(b, {"w": None, "r": []})
            st["w"] = tok
            st["r"] = []
        return tok

    def op(self, eng, fn, reads=(), writes=(), dma=False, sem_name=None):
        return self.group(eng, [fn], reads=reads, writes=writes, dma=dma,
                          sem_name=sem_name)


def build_program():
    nc = bass.Bass("TRN2", target_bir_lowering=False)

    x_hbm = nc.dram_tensor("x", [TOK, D], F32, kind="ExternalInput")
    y_hbm = nc.dram_tensor("y", [TOK, D], F32, kind="ExternalOutput")
    WNAMES = ("wq", "wk", "wv", "wc", "wr", "wk2", "wv2")
    w_hbm = {nm: nc.dram_tensor(nm, [D, D], F32, kind="ExternalInput")
             for nm in WNAMES}
    mur_h = nc.dram_tensor("mur", [128, NCH], F32, kind="ExternalInput")
    muk_h = nc.dram_tensor("muk", [128, NCH], F32, kind="ExternalInput")
    ident_h = nc.dram_tensor("ident", [128, 128], BF16, kind="ExternalInput")

    ctx = ExitStack()
    e = ctx.enter_context
    with ctx:
        wsb = {nm: e(nc.sbuf_tensor(f"W_{nm}", [128, NCH * D], BF16)) for nm in WNAMES}
        wstage = e(nc.sbuf_tensor("wstage", [128, 2 * D], F32))
        lnsc = wstage
        x_tm = [e(nc.sbuf_tensor(f"x_tm{i}", [128, NSUB * D], F32)) for i in range(2)]
        xn_tm = e(nc.sbuf_tensor("xn_tm", [128, NSUB * D], BF16))
        xnf = e(nc.sbuf_tensor("xnf", [128, NCH * W], BF16))
        kbuf = e(nc.sbuf_tensor("kbuf", [128, NCH * W], BF16))
        kvb = e(nc.sbuf_tensor("kvb", [128, NCH * W], BF16))
        ckb = e(nc.sbuf_tensor("ckb", [128, NCH * W], BF16))
        ckv = e(nc.sbuf_tensor("ckv", [128, NCH * W], BF16))
        v16 = e(nc.sbuf_tensor("v16", [128, NCH * 256], BF16))
        rk16 = e(nc.sbuf_tensor("rk16", [128, NCH * 256], BF16))
        tq16 = e(nc.sbuf_tensor("tq16", [128, NCH * 256], BF16))
        tr1p = e(nc.sbuf_tensor("tr1p", [128, NCH * 256], BF16))
        o16 = e(nc.sbuf_tensor("o16", [128, NCH * 256], BF16))
        d1b = e(nc.sbuf_tensor("d1b", [128, NCH * 256], BF16))
        xr16 = e(nc.sbuf_tensor("xr16", [128, NCH * 256], BF16))
        xk16 = e(nc.sbuf_tensor("xk16", [128, NCH * 256], BF16))
        ka16 = e(nc.sbuf_tensor("ka16", [128, NCH * 256], BF16))
        dl16 = e(nc.sbuf_tensor("dl16", [128, NCH * 256], BF16))
        sqsc = e(nc.sbuf_tensor("sqsc", [128, D], BF16))
        ones = e(nc.sbuf_tensor("ones", [128, W], BF16))
        stat = e(nc.sbuf_tensor("stat", [128, 12], F32))
        murs = e(nc.sbuf_tensor("murs", [128, NCH], F32))
        muks = e(nc.sbuf_tensor("muks", [128, NCH], F32))
        idents = e(nc.sbuf_tensor("idents", [128, 128], BF16))
        consts = e(nc.sbuf_tensor("consts", [128, 4], F32))

        psh = [e(nc.psum_tensor(f"psh{i}", [128, 1024], F32)) for i in range(3)]
        pstx = e(nc.psum_tensor("pstx", [128, 2048], BF16))
        pstd = pstx

        sems = {eng: e(nc.semaphore(f"sem_{eng}")) for eng in ENGS}
        sems["sx0"] = e(nc.semaphore("sem_sx0"))
        sems["sx1"] = e(nc.semaphore("sem_sx1"))
        for i in range(4):
            sems[f"sw{i}"] = e(nc.semaphore(f"sem_sw{i}"))
        sem_w0 = e(nc.semaphore("sem_w0"))
        sem_w1 = e(nc.semaphore("sem_w1"))
        sem_w2 = e(nc.semaphore("sem_w2"))
        sem_su = e(nc.semaphore("sem_su"))

        with nc.Block() as blk:
            @blk.sync
            def _(sync):
                sync.dma_start(out=murs[:, :], in_=mur_h[:, :]).then_inc(sem_w2, 16)
                sync.dma_start(out=muks[:, :], in_=muk_h[:, :]).then_inc(sem_w2, 16)
                sync.dma_start(out=idents[:, :], in_=ident_h[:, :]).then_inc(sem_w2, 16)

            @blk.vector
            def _(vector):
                vector.memset(consts[:, 0:1], ATT_EPS)
                nc.const_aps.register(F32, ATT_EPS, consts[:, 0:1])
                vector.memset(consts[:, 1:2], LN_EPS)
                nc.const_aps.register(F32, LN_EPS, consts[:, 1:2])
                vector.memset(consts[:, 2:3], 1.0)
                nc.const_aps.register(F32, 1.0, consts[:, 2:3])
                vector.memset(ones[:, :], 1.0)
                for buf in (xnf, kbuf, kvb, ckb, ckv):
                    vector.memset(buf[:, :], 0.0)

        em = Emitter(sems)

        def fm_data(buf):
            return bass.AP(buf, 4, [[NCH * W, 128], [W, NCH], [1, 256]])

        def fm_shift(buf):
            return bass.AP(buf, 3, [[NCH * W, 128], [W, NCH], [1, 256]])

        def fm_carry(buf):
            return bass.AP(buf, 3, [[NCH * W, 128], [W, NCH], [1, 1]])

        def fm_zc(buf):
            return bass.AP(buf, 2, [[NCH * W, 128], [W, NCH], [1, 2]])

        def fm_last(buf):
            return bass.AP(buf, W - 1, [[NCH * W, 128], [W, NCH], [1, 1]])

        def pl3(buf):
            return bass.AP(buf, 0, [[NCH * 256, 128], [256, NCH], [1, 256]])

        def ps3(ps, n=4):
            return bass.AP(ps, 0, [[1024, 128], [256, n], [1, 256]])

        ps_rr = [0]

        def mm_half(wname, rhs_src, rhs_reads, h):
            pi = ps_rr[0] % 3
            ps_rr[0] += 1
            ps = psh[pi]
            pname = f"psh{pi}"
            fns = []
            for ei in range(4):
                ech = h * 4 + ei
                for d in range(NCH):
                    fns.append(lambda e, ps=ps, ei=ei, ech=ech, d=d, wname=wname:
                               e.matmul(
                                   ps[:, ei * 256:(ei + 1) * 256],
                                   wsb[wname][:, d * D + ech * 128:
                                              d * D + ech * 128 + 128],
                                   rhs_src(d),
                                   start=(d == 0), stop=(d == NCH - 1)))
            em.group("tensor", fns, reads=rhs_reads + (f"W{wname}",),
                     writes=(pname,))
            return pname

        def mm_proj(wname, rhs_src, rhs_reads):
            return [mm_half(wname, rhs_src, rhs_reads, h) for h in range(2)]

        def fe_sync(g):
            xb = x_tm[g % 2]
            xbn = f"x{g % 2}"
            wrs = (xbn, "x1a", "x1b") if g == 1 else (xbn,)
            em.op("sync", lambda e, g=g, xb=xb: e.dma_start(
                out=bass.AP(xb, 0, [[NSUB * D, 128], [D, NSUB], [1, D]]),
                in_=bass.AP(x_hbm, g * G * D, [[D, 128], [128 * D, NSUB], [1, D]])),
                reads=(), writes=wrs, dma=True, sem_name=f"sx{g % 2}")

        def fe_scalar_sumsq(g):
            xb = x_tm[g % 2]
            xbn = f"x{g % 2}"
            for su in range(NSUB):
                em.op("scalar", lambda e, xb=xb, su=su: e.activation(
                    sqsc[:, :], xb[:, su * D:(su + 1) * D], AF.Square,
                    accum_out=stat[:, 2 + su:3 + su]),
                    reads=(xbn,), writes=(f"ssq{su}", "sqsc"))

        def fe_vec_ln(g):
            xb = x_tm[g % 2]
            xbn = f"x{g % 2}"
            for su in range(NSUB):
                em.op("vector", lambda e, xb=xb, su=su: e.reduce_sum(
                    stat[:, 0 + su:1 + su], xb[:, su * D:(su + 1) * D], axis=AX.X),
                    reads=(xbn,), writes=(f"sum{su}",))
            em.op("vector", lambda e: e.tensor_scalar_mul(
                stat[:, 4:6], stat[:, 0:2], 1.0 / D),
                reads=("sum0", "sum1"), writes=("mean",))
            em.op("vector", lambda e: e.tensor_mul(
                stat[:, 6:8], stat[:, 4:6], stat[:, 4:6]),
                reads=("mean",), writes=("msq",))
            em.op("vector", lambda e: e.scalar_tensor_tensor(
                stat[:, 8:10], stat[:, 2:4], 1.0 / D, stat[:, 6:8],
                ALU.mult, ALU.subtract),
                reads=("ssq0", "ssq1", "msq"), writes=("var",))

        def fe_scalar_rstd(g):
            em.op("scalar", lambda e: e.activation(
                stat[:, 10:12], stat[:, 8:10], AF.Ln, bias=LN_EPS),
                reads=("var",), writes=("rstd",))
            em.op("scalar", lambda e: e.activation(
                stat[:, 10:12], stat[:, 10:12], AF.Exp, scale=-0.5),
                reads=("rstd",), writes=("rstd",))

        def fe_vec_xn(g):
            xb = x_tm[g % 2]
            xbn = f"x{g % 2}"
            for su in range(NSUB):
                em.op("vector", lambda e, xb=xb, su=su: e.tensor_scalar(
                    xn_tm[:, su * D:(su + 1) * D], xb[:, su * D:(su + 1) * D],
                    stat[:, 4 + su:5 + su], stat[:, 10 + su:11 + su],
                    ALU.subtract, ALU.mult),
                    reads=(xbn, "mean", "rstd"), writes=("xn_tm",))

        def fe_tensor_tpose(g):
            fns = []
            for c in range(NCH):
                for su in range(NSUB):
                    fns.append(lambda e, c=c, su=su: e.transpose(
                        pstx[:, c * 256 + su * 128: c * 256 + su * 128 + 128],
                        xn_tm[:, su * D + c * 128: su * D + c * 128 + 128],
                        idents[:, :]))
            em.group("tensor", fns, reads=("xn_tm",), writes=("pstx",))

        def front_end_start(g):
            fe_sync(g)
            fe_scalar_sumsq(g)
            fe_vec_ln(g)
            fe_scalar_rstd(g)
            fe_vec_xn(g)
            fe_tensor_tpose(g)

        front_end_start(0)

        def hist_evac(g):
            if g % GSEQ == 0:
                em.op("vector", lambda e: e.memset(fm_zc(xnf), 0.0),
                      reads=(), writes=("xnf",))
            else:
                em.op("scalar", lambda e: e.copy(fm_carry(xnf), fm_last(xnf)),
                      reads=(), writes=("xnf",))
            em.op("scalar", lambda e: e.copy(
                fm_data(xnf),
                bass.AP(pstx, 0, [[2048, 128], [256, NCH], [1, 256]])),
                reads=("pstx",), writes=("xnf",))

        WLOAD = ("wv", "wk", "wq", "wr", "wk2", "wv2", "wc")
        wslot_ap = [wstage[:, 0:D], wstage[:, D:2 * D],
                    x_tm[1][:, 0:D], x_tm[1][:, D:2 * D]]
        wslot_nm = ["wsl0", "wsl1", "x1a", "x1b"]
        k = 0
        for nm in WLOAD:
            scalef = None
            for c in range(NCH):
                slot = k % 4
                em.op("sync", lambda e, nm=nm, c=c, slot=slot: e.dma_start(
                    out=wslot_ap[slot],
                    in_=w_hbm[nm][c * 128:(c + 1) * 128, :]),
                    reads=(), writes=(wslot_nm[slot],), dma=True,
                    sem_name=f"sw{slot}")
                wr = (f"W{nm}",) if c == NCH - 1 else (wslot_nm[slot],)
                src = wslot_ap[slot]
                dst = wsb[nm][:, c * D:(c + 1) * D]
                if k % 2 == 0:
                    if scalef is None:
                        em.op("vector", lambda e, src=src, dst=dst: e.tensor_copy(
                            dst, src), reads=(wslot_nm[slot],), writes=wr)
                    else:
                        em.op("vector", lambda e, src=src, dst=dst, scalef=scalef:
                              e.tensor_scalar_mul(dst, src, scalef),
                              reads=(wslot_nm[slot],), writes=wr)
                else:
                    if scalef is None:
                        em.op("scalar", lambda e, src=src, dst=dst: e.copy(
                            dst, src), reads=(wslot_nm[slot],), writes=wr)
                    else:
                        em.op("scalar", lambda e, src=src, dst=dst, scalef=scalef:
                              e.activation(dst, src, AF.Copy, scale=scalef),
                              reads=(wslot_nm[slot],), writes=wr)
                k += 1
        for nm_ in ("wsl0", "wsl1"):
            st_ = em.bufs.get(nm_)
            if st_:
                al = em.bufs.setdefault("lnsc", {"w": None, "r": []})
                if st_["w"]:
                    al["r"].append(st_["w"])
                al["r"].extend(st_["r"])

        hist_evac(0)

        for g in range(NG):
            gi = g % GSEQ
            xb = x_tm[g % 2]
            xbn = f"x{g % 2}"
            if g + 1 < NG:
                fe_sync(g + 1)

            em.op("vector", lambda e: e.tensor_sub(
                pl3(d1b), fm_data(xnf), fm_shift(xnf)),
                reads=("xnf",), writes=("d1",))
            fns = [lambda e, c=c: e.scalar_tensor_tensor(
                xr16[:, c * 256:(c + 1) * 256], d1b[:, c * 256:(c + 1) * 256],
                murs[:, c:c + 1],
                bass.AP(xnf, c * W + 3, [[NCH * W, 128], [1, 256]]),
                ALU.mult, ALU.add) for c in range(NCH)]
            em.group("vector", fns, reads=("d1", "xnf"), writes=("xr",))
            fns = [lambda e, c=c: e.scalar_tensor_tensor(
                xk16[:, c * 256:(c + 1) * 256], d1b[:, c * 256:(c + 1) * 256],
                muks[:, c:c + 1],
                bass.AP(xnf, c * W + 3, [[NCH * W, 128], [1, 256]]),
                ALU.mult, ALU.add) for c in range(NCH)]
            em.group("vector", fns, reads=("d1", "xnf"), writes=("xk",))

            xnf_rhs = lambda d: bass.AP(xnf, d * W + 4, [[NCH * W, 128], [1, 256]])
            pv = mm_proj("wv", xnf_rhs, ("xnf",))
            for h in range(2):
                pn = pv[h]
                em.op("scalar", lambda e, pn=pn, h=h: e.copy(
                    v16[:, h * 1024:(h + 1) * 1024], psh[int(pn[-1])][:, :]),
                    reads=(pn,), writes=("v16",))
            pk = mm_proj("wk", xnf_rhs, ("xnf",))
            if gi == 0:
                em.op("vector", lambda e: e.memset(fm_zc(kbuf), 0.0),
                      reads=(), writes=("kbuf",))
                em.op("vector", lambda e: e.memset(fm_zc(kvb), 0.0),
                      reads=(), writes=("kvb",))
            else:
                em.op("scalar", lambda e: e.copy(fm_carry(kbuf), fm_last(ckb)),
                      reads=("ckb",), writes=("kbuf",))
                em.op("vector", lambda e: e.tensor_copy(fm_carry(kvb), fm_last(ckv)),
                      reads=("ckv",), writes=("kvb",))
            for h in range(2):
                pn = pk[h]
                em.op("scalar", lambda e, pn=pn, h=h: e.activation(
                    bass.AP(kbuf, h * 4 * W + 4, [[NCH * W, 128], [W, 4], [1, 256]]),
                    ps3(psh[int(pn[-1])]), AF.Exp),
                    reads=(pn,), writes=("kbuf",))
            em.op("vector", lambda e: e.tensor_mul(
                fm_data(kvb), fm_data(kbuf), pl3(v16)),
                reads=("kbuf", "v16"), writes=("kvb",))
            for c in range(NCH):
                em.op("vector", lambda e, c=c: e.tensor_tensor_scan(
                    out=ckb[:, c * W + 2:(c + 1) * W],
                    data0=ones[:, 0:W - 2], data1=kbuf[:, c * W + 2:(c + 1) * W],
                    initial=0.0, op0=ALU.mult, op1=ALU.add),
                    reads=("kbuf",) if c == 0 else (), writes=(f"ckb{c}",))
            for c in range(NCH):
                em.op("vector", lambda e, c=c: e.tensor_tensor_scan(
                    out=ckv[:, c * W + 2:(c + 1) * W],
                    data0=ones[:, 0:W - 2], data1=kvb[:, c * W + 2:(c + 1) * W],
                    initial=0.0, op0=ALU.mult, op1=ALU.add),
                    reads=("kvb",) if c == 0 else (), writes=(f"ckv{c}",))

            pq = mm_proj("wq", xnf_rhs, ("xnf",))
            for h in range(2):
                pn = pq[h]
                em.op("scalar", lambda e, pn=pn, h=h: e.activation(
                    tq16[:, h * 1024:(h + 1) * 1024], psh[int(pn[-1])][:, :],
                    AF.Exp, scale=-1.0),
                    reads=(pn,), writes=("tq",))
            em.op("scalar", lambda e: e.activation(
                tq16[:, :], tq16[:, :], AF.Ln, bias=1.0),
                reads=(), writes=("tq",))
            em.op("scalar", lambda e: e.activation(
                tq16[:, :], tq16[:, :], AF.Exp, scale=-1.0),
                reads=(), writes=("tq",))
            pr = mm_proj("wr", lambda d: xr16[:, d * 256:(d + 1) * 256], ("xr",))
            for h in range(2):
                pn = pr[h]
                em.op("scalar", lambda e, pn=pn, h=h: e.activation(
                    tr1p[:, h * 1024:(h + 1) * 1024], psh[int(pn[-1])][:, :],
                    AF.Exp, scale=-1.0),
                    reads=(pn,), writes=("tr1p",))
            em.op("scalar", lambda e: e.activation(
                tr1p[:, :], tr1p[:, :], AF.Ln, bias=1.0),
                reads=(), writes=("tr1p",))
            em.op("scalar", lambda e: e.activation(
                tr1p[:, :], tr1p[:, :], AF.Exp, scale=-1.0),
                reads=(), writes=("tr1p",))
            pk2 = []
            for h in range(2):
                pk2.append(mm_half("wk2", lambda d: xk16[:, d * 256:(d + 1) * 256],
                                   ("xk",), h))
                pn = pk2[h]
                em.op("vector", lambda e, pn=pn, h=h: e.tensor_scalar_max(
                    ka16[:, h * 1024:(h + 1) * 1024], psh[int(pn[-1])][:, :], 0.0),
                    reads=(pn,), writes=("ka",))
                em.op("vector", lambda e, h=h: e.tensor_mul(
                    ka16[:, h * 1024:(h + 1) * 1024],
                    ka16[:, h * 1024:(h + 1) * 1024],
                    ka16[:, h * 1024:(h + 1) * 1024]),
                    reads=(), writes=("ka",))

            if g + 1 < NG:
                fe_scalar_sumsq(g + 1)
                fe_vec_ln(g + 1)

            em.op("scalar", lambda e: e.activation(
                pl3(lnsc), fm_data(ckb), AF.Ln, bias=ATT_EPS),
                reads=("ckb0", "ckb1", "ckb2", "ckb3", "ckb4", "ckb5", "ckb6", "ckb7",), writes=("ckb", "lnsc"))
            em.op("scalar", lambda e: e.activation(
                rk16[:, :], lnsc[:, :], AF.Exp, scale=-1.0),
                reads=("lnsc",), writes=("rk",))
            if g + 1 < NG:
                fe_scalar_rstd(g + 1)

            em.op("vector", lambda e: e.tensor_mul(
                pl3(d1b), fm_data(ckv), pl3(rk16)),
                reads=("ckv0", "ckv1", "ckv2", "ckv3", "ckv4", "ckv5", "ckv6", "ckv7", "rk"), writes=("ckv", "d1"))
            em.op("vector", lambda e: e.tensor_mul(
                pl3(o16), pl3(tq16), pl3(d1b)),
                reads=("tq", "d1"), writes=("o16",))
            if g + 1 < NG:
                fe_vec_xn(g + 1)

            pf = []
            for h in range(2):
                pf.append(mm_half("wv2", lambda d: ka16[:, d * 256:(d + 1) * 256],
                                  ("ka",), h))
                fn_ = pf[h]
                em.op("vector", lambda e, fn_=fn_, h=h: e.tensor_mul(
                    tq16[:, h * 1024:(h + 1) * 1024], psh[int(fn_[-1])][:, :],
                    tr1p[:, h * 1024:(h + 1) * 1024]),
                    reads=(fn_, "tr1p"), writes=("tq",))
            if g + 1 < NG:
                fe_tensor_tpose(g + 1)
                hist_evac(g + 1)
            for h in range(2):
                an_ = mm_half("wc", lambda d: o16[:, d * 256:(d + 1) * 256],
                              ("o16",), h)
                em.op("vector", lambda e, an_=an_, h=h: e.tensor_add(
                    dl16[:, h * 1024:(h + 1) * 1024],
                    tq16[:, h * 1024:(h + 1) * 1024], psh[int(an_[-1])][:, :]),
                    reads=(an_, "tq"), writes=("dl",))

            fns = []
            for c in range(NCH):
                for su in range(NSUB):
                    fns.append(lambda e, c=c, su=su: e.transpose(
                        pstd[:, su * D + c * 128: su * D + c * 128 + 128],
                        dl16[:, c * 256 + su * 128: c * 256 + su * 128 + 128],
                        idents[:, :]))
            em.group("tensor", fns, reads=("dl",), writes=("pstx",))
            em.op("vector", lambda e, xb=xb: e.tensor_add(
                xb[:, :], xb[:, :], pstd[:, :]),
                reads=("pstx",), writes=(xbn,))
            em.op("gpsimd", lambda e, g=g, xb=xb: e.dma_start(
                out=bass.AP(y_hbm, g * G * D, [[D, 128], [128 * D, NSUB], [1, D]]),
                in_=bass.AP(xb, 0, [[NSUB * D, 128], [D, NSUB], [1, D]])),
                reads=(xbn,), writes=(), dma=True)

        total_store = em.cnt["gpsimd"]

        with nc.Block() as blk:
            @blk.sync
            def _(eng):
                for f in em.plan["sync"]:
                    f(eng)

            @blk.tensor
            def _(eng):
                eng.wait_ge(sem_w2, 48)
                for f in em.plan["tensor"]:
                    f(eng)

            @blk.vector
            def _(eng):
                for f in em.plan["vector"]:
                    f(eng)

            @blk.scalar
            def _(eng):
                for f in em.plan["scalar"]:
                    f(eng)

            @blk.gpsimd
            def _(eng):
                for f in em.plan["gpsimd"]:
                    f(eng)
                eng.wait_ge(sems["gpsimd"], total_store)

    return nc


_NC_CACHE = [None]


def kernel(L0_attn_X, L0_attn_Wq, L0_attn_Wk, L0_attn_Wv, L0_attn_Wc,
           L0_ffn_W_r, L0_ffn_W_k, L0_ffn_W_v, L0_ffn_mu_r, L0_ffn_mu_k):
    X = np.ascontiguousarray(np.asarray(L0_attn_X, dtype=np.float32))
    weights = {
        "wq": np.ascontiguousarray(np.asarray(L0_attn_Wq, np.float32)),
        "wk": np.ascontiguousarray(np.asarray(L0_attn_Wk, np.float32)),
        "wv": np.ascontiguousarray(np.asarray(L0_attn_Wv, np.float32)),
        "wc": np.ascontiguousarray(np.asarray(L0_attn_Wc, np.float32)),
        "wr": np.ascontiguousarray(np.asarray(L0_ffn_W_r, np.float32).T),
        "wk2": np.ascontiguousarray(np.asarray(L0_ffn_W_k, np.float32).T),
        "wv2": np.ascontiguousarray(np.asarray(L0_ffn_W_v, np.float32).T),
    }
    mur = np.ascontiguousarray(np.asarray(L0_ffn_mu_r, np.float32).reshape(NCH, 128).T)
    muk = np.ascontiguousarray(np.asarray(L0_ffn_mu_k, np.float32).reshape(NCH, 128).T)
    ident = np.eye(128, dtype=ml_dtypes.bfloat16)

    if _NC_CACHE[0] is None:
        _NC_CACHE[0] = build_program()
    nc = _NC_CACHE[0]

    in_maps = []
    for i in range(NCORES):
        xi = np.ascontiguousarray(
            X[SEQ_PER_CORE * i: SEQ_PER_CORE * (i + 1)].reshape(TOK, D))
        m = {"x": xi, "mur": mur, "muk": muk, "ident": ident}
        m.update(weights)
        in_maps.append(m)

    res = run_bass_kernel_spmd(nc, in_maps, core_ids=list(range(NCORES)))
    out = np.empty((B, L, D), dtype=np.float32)
    for i in range(NCORES):
        out[SEQ_PER_CORE * i: SEQ_PER_CORE * (i + 1)] = np.asarray(
            res.results[i]["y"]).reshape(SEQ_PER_CORE, L, D)
    return out
